# version 08
# speedup vs baseline: 1.1743x; 1.0108x over previous
import contextlib

import numpy as np
import ml_dtypes

import concourse.bass as bass
import concourse.mybir as mybir
from concourse.bass_utils import run_bass_kernel_spmd

F32 = mybir.dt.float32
BF16 = mybir.dt.bfloat16
FP8 = mybir.dt.float8e4
U8 = mybir.dt.uint8
I32 = mybir.dt.int32
AF = mybir.ActivationFunctionType
ALU = mybir.AluOpType
AX = mybir.AxisListType

HID = 2048
NH = 8
HD = 128
HCOLS = NH * HD
FFI = 6144
SCALE = 128.0 ** -0.5
EPS_LN = 1e-5
EPS_RMS = 1e-5
EPS_L2 = 1e-6

ENGINES = ("sync", "tensor", "vector", "scalar", "gpsimd")


class EB:

    def __init__(self, nc):
        self.nc = nc
        self.streams = {e: [] for e in ENGINES}
        self.cnt = {}
        self.lastw = {e: {} for e in ENGINES}
        self.sb = {}
        self.ps = {}
        self.pbfree = {}

    def sem(self, *names):
        for n in names:
            self.cnt[n] = 0

    def op(self, eng, fn):
        self.streams[eng].append(fn)

    def need(self, eng, *toks):
        for t in toks:
            if t is None:
                continue
            sem, val = t
            if val is None or val <= 0:
                continue
            if self.lastw[eng].get(sem, 0) >= val:
                continue
            self.lastw[eng][sem] = val
            self.op(eng, lambda E, S, sem=sem, val=val: E.wait_ge(S[sem], val))

    def bank(self, *names):
        for n in names:
            self.need("tensor", self.pbfree.get(n))

    def do(self, eng, fn, sem=None, k=1):
        if sem is None:
            self.op(eng, fn)
            return None
        self.cnt[sem] += k
        val = self.cnt[sem]
        self.op(eng, lambda E, S, fn=fn, sem=sem, k=k: fn(E, S).then_inc(S[sem], k))
        return (sem, val)

    def dma(self, eng, sem, out_fn, in_fn, transpose=False):
        self.cnt[sem] += 16
        val = self.cnt[sem]
        if transpose:
            self.op(eng, lambda E, S, o=out_fn, i=in_fn, sem=sem: E.dma_start(out=o(), in_=i(), transpose=True).then_inc(S[sem], 16))
        else:
            self.op(eng, lambda E, S, o=out_fn, i=in_fn, sem=sem: E.dma_start(out=o(), in_=i()).then_inc(S[sem], 16))
        return (sem, val)

    def tok(self, sem):
        return (sem, self.cnt[sem])


def tile3(t, p0, c0, nh, tl, rowstride):
    return bass.AP(t, p0 * rowstride + c0, [[rowstride, 128], [rowstride * 128, nh], [1, tl]])


def kt3(t, cols, nkt=16):
    return bass.AP(t, 0, [[cols, 128], [cols * 128, nkt], [1, cols]])


def t3(tens):
    return bass.AP(tens, 0, [[512, 128], [128, 4], [1, 128]])


def t3b(pbank):
    return bass.AP(pbank, 0, [[512, 128], [128, 4], [1, 128]])


def bc3(tens, off=0):
    w = tens.shape[1]
    return bass.AP(tens, off, [[w, 128], [1, 4], [0, 128]])


def red3(tens, off=0):
    w = tens.shape[1]
    return bass.AP(tens, off, [[w, 128], [1, 4], [1, 1]])


def build_program(T=4096):
    nc = bass.Bass("TRN2", target_bir_lowering=False, debug=False)
    NTA = T // 1024
    TO = T // 2

    inp = {}

    def ein(name, shape, dt=F32):
        inp[name] = nc.dram_tensor(name, list(shape), dt, kind="ExternalInput")

    ein("h", (T, HID)); ein("xo", (TO, HID))
    for n in ("wq", "wk", "wv", "ww"):
        ein(n, (HID, HCOLS))
    ein("cq", (128, 32)); ein("ck", (128, 32)); ein("cv", (128, 32))
    ein("wf1", (HID, 128)); ein("wf2", (128, HCOLS)); ein("wb", (HID, NH))
    ein("wg1", (HID, 128)); ein("wg2", (128, HCOLS))
    ein("bg2b", (128, HCOLS)); ein("normwb", (128, 512))
    ein("wo", (HID, HID))
    ein("gw", (HID, FFI)); ein("uw", (HID, FFI)); ein("dw", (FFI, HID))
    ein("gsc", (128, FFI // 128)); ein("usc", (128, FFI // 128)); ein("dsc", (128, HID // 128))
    ein("ident", (128, 128)); ein("trineg", (128, 128)); ein("maskst", (128, 128))
    ein("onesrow", (1, 128)); ein("negcol", (128, 1))
    ein("offs", (1, 8), I32)
    ein("cvals", (128, 4))
    out_t = nc.dram_tensor("out", [TO, HID], F32, kind="ExternalOutput")

    scr = {}

    def sdram(name, shape, dt=F32):
        scr[name] = nc.dram_tensor(name, list(shape), dt)

    for n in ("sq", "sk", "sv", "sw"):
        sdram(n, (HCOLS, T))
    sdram("sg", (T, HCOLS)); sdram("sgate", (T, HCOLS)); sdram("sbeta", (T, NH))
    sdram("sxnh", (T, HID), BF16); sdram("sxnl", (T, HID), BF16)
    for wn, r, c in (("wq", HID, HCOLS), ("wk", HID, HCOLS), ("wv", HID, HCOLS), ("ww", HID, HCOLS),
                     ("wo", HID, HID), ("wf1", HID, 128), ("wg1", HID, 128),
                     ("wf2", 128, HCOLS), ("wg2", 128, HCOLS), ("wb", HID, NH)):
        sdram("h_" + wn, (r, c), BF16)
        sdram("l_" + wn, (r, c), BF16)
    sdram("q_gw", (HID, FFI), BF16); sdram("q_uw", (HID, FFI), BF16); sdram("q_dw", (FFI, HID), BF16)
    sdram("xnoh", (TO, HID), BF16); sdram("xnol", (TO, HID), BF16)
    sdram("fuh", (FFI, TO), BF16); sdram("ful", (FFI, TO), BF16)
    sdram("ffno", (HID, TO))
    sdram("cc_in", (HCOLS, T)); sdram("cc_out", (2 * HCOLS, T))

    e = EB(nc)
    semnames = ["ld", "hld", "hld1", "spx1", "dw2", "dw3", "wl2", "wl3", "fw2", "fw3", "ww2", "ww3", "xnl", "pw0", "pw1", "ph0", "ph1", "ql0", "ql1",
                "wl0", "wl1", "bl00", "bl01", "bl10", "bl11", "fxl", "fw0", "fw1",
                "fcl", "dw0", "dw1", "odl", "ww0", "ww1", "oxl", "ofl",
                "sp", "spx", "spg", "sw0", "sw1", "sq0", "sq1", "mm", "ve", "sc", "gp", "cc", "so"]
    e.sem(*semnames)
    e.T, e.NTA, e.TO = T, NTA, TO

    sems = {}
    with contextlib.ExitStack() as es:
        for s in semnames:
            sems[s] = es.enter_context(nc.semaphore(s))
        for i in range(8):
            e.ps["pb%d" % i] = es.enter_context(nc.psum_tensor("pb%d" % i, [128, 512], F32))
        e.r_own = es.enter_context(nc.sync.register("r_own"))
        e.r_tmp = es.enter_context(nc.sync.register("r_tmp"))

        gl = [("c_ident", (128, 128), F32), ("c_trineg", (128, 128), F32),
              ("c_mask", (128, 128), F32), ("c_onesrow", (1, 128), F32),
              ("c_negcol", (128, 1), F32), ("c_bg2b", (128, HCOLS), F32),
              ("c_normwb", (128, 512), F32),
              ("c_cq", (128, 32), F32), ("c_ck", (128, 32), F32), ("c_cv", (128, 32), F32),
              ("c_gsc", (128, FFI // 128), F32), ("c_usc", (128, FFI // 128), F32),
              ("c_dsc", (128, HID // 128), F32), ("c_offs", (1, 8), I32),
              ("c_cvals", (128, 4), F32)]
        for nm, sh, dt in gl:
            e.sb[nm] = es.enter_context(nc.sbuf_tensor(nm, list(sh), dt))
        srcmap = {"c_mask": "maskst"}
        for nm, _, _ in gl:
            sn = srcmap.get(nm, nm[2:])
            tc = e.dma("sync", "ld", (lambda nm=nm: e.sb[nm][:]), (lambda sn=sn: inp[sn].ap()))
        e.t_const = tc
        for _eng in ENGINES:
            e.need(_eng, tc)
        nc.const_aps.register(F32, EPS_LN, e.sb["c_cvals"][:, 0:1])
        nc.const_aps.register(F32, EPS_L2, e.sb["c_cvals"][:, 1:2])
        nc.const_aps.register(F32, 1.0, e.sb["c_cvals"][:, 2:3])
        nc.const_aps.register(F32, -1.0, e.sb["c_cvals"][:, 3:4])

        pes = contextlib.ExitStack()
        emit_presplit(e, inp, scr, pes, nc)
        emit_phaseA(e, inp, scr, nc)
        pes.close()
        emit_phaseB(e, inp, scr, nc)
        emit_collective(e, scr)
        emit_ffn(e, inp, scr, nc)
        emit_wo_final(e, inp, scr, nc, out_t)

        with nc.Block() as block:
            @block.sync
            def _(E):
                for fn in e.streams["sync"]:
                    fn(E, sems)

            @block.tensor
            def _(E):
                for fn in e.streams["tensor"]:
                    fn(E, sems)

            @block.vector
            def _(E):
                for fn in e.streams["vector"]:
                    fn(E, sems)

            @block.scalar
            def _(E):
                for fn in e.streams["scalar"]:
                    fn(E, sems)

            @block.gpsimd
            def _(E):
                for fn in e.streams["gpsimd"]:
                    fn(E, sems)
    return nc


def emit_presplit(e, inp, scr, es, nc):
    sb = {}
    for i in range(2):
        for nm, dt in (("w32", F32), ("whi", BF16), ("wlo", BF16), ("f8", FP8), ("u8", U8)):
            sb["%s%d" % (nm, i)] = es.enter_context(nc.sbuf_tensor("p_%s%d" % (nm, i), [128, 1024], dt))
    e.psb = sb

    jobs = []
    for wn, rows, cols in (("wf1", HID, 128), ("wg1", HID, 128), ("wf2", 128, HCOLS),
                           ("wg2", 128, HCOLS), ("wb", HID, NH),
                           ("wq", HID, HCOLS), ("wk", HID, HCOLS), ("wv", HID, HCOLS),
                           ("ww", HID, HCOLS), ("wo", HID, HID)):
        for r0 in range(0, rows, 128):
            for c0 in range(0, cols, 1024):
                jobs.append((wn, r0, c0, min(1024, cols - c0)))
    e.split_jobs = jobs
    e.split_prev = {0: None, 1: None}
    e.slot_prev = {0: None, 1: None}
    e.split_idx = 0
    e.t_wsplit = {}

    qjobs = []
    for wn, dst, rows, cols in (("gw", "q_gw", HID, FFI), ("uw", "q_uw", HID, FFI),
                                ("dw", "q_dw", FFI, HID)):
        for r0 in range(0, rows, 128):
            for c0 in range(0, cols, 1024):
                qjobs.append((wn, r0, c0, min(1024, cols - c0), dst))
    e.quant_jobs = qjobs
    e.quant_prev = {0: None, 1: None}
    e.quant_idx = 0


def emit_split_chunk(e, inp, scr, n):
    sb = e.psb
    for _ in range(n):
        if e.split_idx >= len(e.split_jobs):
            return
        wn, r0, c0, w = e.split_jobs[e.split_idx]
        s = e.split_idx % 2
        e.split_idx += 1
        if e.slot_prev[s] is not None:
            e.need("sync", e.slot_prev[s]); e.need("gpsimd", e.slot_prev[s]); e.need("vector", e.slot_prev[s])
        t32 = e.dma("sync", "pw%d" % s, lambda s=s, w=w: sb["w32%d" % s][:, 0:w],
                    lambda wn=wn, r0=r0, c0=c0, w=w: inp[wn][r0:r0 + 128, c0:c0 + w])
        thi = e.dma("gpsimd", "ph%d" % s, lambda s=s, w=w: sb["whi%d" % s][:, 0:w],
                    lambda wn=wn, r0=r0, c0=c0, w=w: inp[wn][r0:r0 + 128, c0:c0 + w])
        e.need("vector", t32, thi)
        tv = e.do("vector", lambda E, S, s=s, w=w: E.tensor_tensor(sb["wlo%d" % s][:, 0:w], sb["w32%d" % s][:, 0:w], sb["whi%d" % s][:, 0:w], ALU.subtract), "ve")
        e.need("gpsimd", tv)
        e.dma("gpsimd", "sw%d" % s, lambda wn=wn, r0=r0, c0=c0, w=w: scr["h_" + wn][r0:r0 + 128, c0:c0 + w],
              lambda s=s, w=w: sb["whi%d" % s][:, 0:w])
        e.split_prev[s] = e.dma("gpsimd", "sw%d" % s, lambda wn=wn, r0=r0, c0=c0, w=w: scr["l_" + wn][r0:r0 + 128, c0:c0 + w],
                                lambda s=s, w=w: sb["wlo%d" % s][:, 0:w])
        e.slot_prev[s] = e.split_prev[s]
        e.t_wsplit[wn] = (e.tok("sw0"), e.tok("sw1"))


def emit_quant_chunk(e, inp, scr, n):
    sb = e.psb
    for _ in range(n):
        if e.quant_idx >= len(e.quant_jobs):
            return
        wn, r0, c0, w, dst = e.quant_jobs[e.quant_idx]
        s = e.quant_idx % 2
        e.quant_idx += 1
        if e.slot_prev[s] is not None:
            e.need("sync", e.slot_prev[s]); e.need("scalar", e.slot_prev[s])
            e.need("vector", e.slot_prev[s]); e.need("gpsimd", e.slot_prev[s])
        t32 = e.dma("sync", "ql%d" % s, lambda s=s, w=w: sb["w32%d" % s][:, 0:w],
                    lambda wn=wn, r0=r0, c0=c0, w=w: inp[wn][r0:r0 + 128, c0:c0 + w])
        e.need("scalar", t32)
        ts1 = e.do("scalar", lambda E, S, s=s, w=w: E.copy(sb["f8%d" % s][:, 0:w], sb["w32%d" % s][:, 0:w]), "sc")
        e.need("vector", ts1)
        tv1 = e.do("vector", lambda E, S, s=s, w=w: E.tensor_copy(sb["u8%d" % s][:, 0:w], sb["f8%d" % s][:, 0:w].bitcast(U8)), "ve")
        e.need("vector", tv1)
        tv2 = e.do("vector", lambda E, S, s=s, w=w: E.tensor_copy(sb["whi%d" % s][:, 0:w], sb["u8%d" % s][:, 0:w].bitcast(FP8)), "ve")
        e.need("gpsimd", tv2)
        e.quant_prev[s] = e.dma("gpsimd", "sq%d" % s, lambda dst=dst, r0=r0, c0=c0, w=w: scr[dst][r0:r0 + 128, c0:c0 + w],
                                lambda s=s, w=w: sb["whi%d" % s][:, 0:w])
        e.slot_prev[s] = e.quant_prev[s]


def emit_ln_block(e, sb, pfx, src_fn, spill_h_fn, spill_l_fn, guard=None, ldsem="hld", spsem="spx"):
    tl = e.dma("sync", ldsem, lambda pfx=pfx: sb[pfx + "h"][:], src_fn)
    e.need("scalar", tl)
    t1 = e.do("scalar", lambda E, S, pfx=pfx: E.activation(sb[pfx + "xnh"][:], sb[pfx + "h"][:], AF.Copy,
                                                 accum_out=sb[pfx + "stat"][:, 0:1]), "sc")
    e.need("scalar", t1)
    t2 = e.do("scalar", lambda E, S, pfx=pfx: E.activation(sb[pfx + "xnl"][:], sb[pfx + "h"][:], AF.Square,
                                                 accum_out=sb[pfx + "stat"][:, 1:2]), "sc")
    e.need("vector", t2)
    e.do("vector", lambda E, S, pfx=pfx: E.tensor_scalar(sb[pfx + "stat"][:, 2:3], sb[pfx + "stat"][:, 0:1], 1.0 / HID, None, ALU.mult), "ve")
    e.need("vector", e.tok("ve"))
    e.do("vector", lambda E, S, pfx=pfx: E.tensor_scalar(sb[pfx + "stat"][:, 6:7], sb[pfx + "stat"][:, 1:2], 1.0 / HID, None, ALU.mult), "ve")
    e.need("vector", e.tok("ve"))
    e.do("vector", lambda E, S, pfx=pfx: E.tensor_tensor(sb[pfx + "stat"][:, 3:4], sb[pfx + "stat"][:, 2:3], sb[pfx + "stat"][:, 2:3], ALU.mult), "ve")
    e.need("vector", e.tok("ve"))
    tv = e.do("vector", lambda E, S, pfx=pfx: E.tensor_tensor(sb[pfx + "stat"][:, 3:4], sb[pfx + "stat"][:, 6:7], sb[pfx + "stat"][:, 3:4], ALU.subtract), "ve")
    e.need("scalar", tv)
    ts = e.do("scalar", lambda E, S, pfx=pfx: E.activation(sb[pfx + "stat"][:, 4:5], sb[pfx + "stat"][:, 3:4], AF.Sqrt, bias=EPS_LN), "sc")
    e.need("vector", ts)
    e.do("vector", lambda E, S, pfx=pfx: E.reciprocal(sb[pfx + "stat"][:, 7:8], sb[pfx + "stat"][:, 4:5]), "ve")
    e.need("vector", e.tok("ve"))
    tv2 = e.do("vector", lambda E, S, pfx=pfx: E.scalar_tensor_tensor(sb[pfx + "stat"][:, 5:6], sb[pfx + "stat"][:, 2:3], -1.0,
                                                             sb[pfx + "stat"][:, 7:8], ALU.mult, ALU.mult), "ve")
    e.need("scalar", tv2, t2, guard)
    ts2 = e.do("scalar", lambda E, S, pfx=pfx: E.activation(sb["ash_xn32"][:], sb[pfx + "h"][:], AF.Identity,
                                                   bias=sb[pfx + "stat"][:, 5:6], scale=sb[pfx + "stat"][:, 7:8]), "sc")
    e.need("vector", ts2)
    e.do("vector", lambda E, S, pfx=pfx: E.tensor_copy(sb[pfx + "xnh"][:], sb["ash_xn32"][:]), "ve")
    e.need("vector", e.tok("ve"))
    tv3 = e.do("vector", lambda E, S, pfx=pfx: E.tensor_tensor(sb[pfx + "xnl"][:], sb["ash_xn32"][:], sb[pfx + "xnh"][:], ALU.subtract), "ve")
    e.need("gpsimd", tv3)
    e.dma("gpsimd", spsem, spill_h_fn, lambda pfx=pfx: sb[pfx + "xnh"][:])
    return e.dma("gpsimd", spsem, spill_l_fn, lambda pfx=pfx: sb[pfx + "xnl"][:]), tv3


def xslc(sb, name, kt, n):
    return sb[name][:, kt * 1024 + n * 512: kt * 1024 + (n + 1) * 512]


def emit_phaseA(e, inp, scr, nc):
    T, NTA = e.T, e.NTA
    with contextlib.ExitStack() as aes:
        sb = {}

        def asb(name, shape, dt=F32):
            sb[name] = aes.enter_context(nc.sbuf_tensor(name, list(shape), dt))

        for s4 in range(2):
            asb("a%d_h" % s4, (128, HID)); asb("a%d_stat" % s4, (128, 8))
            asb("a%d_xnh" % s4, (128, HID), BF16); asb("a%d_xnl" % s4, (128, HID), BF16)
        asb("ash_xn32", (128, HID)); asb("a_mark", (1, 4))
        asb("xnh", (128, 16 * 1024), BF16); asb("xnl", (128, 16 * 1024), BF16)
        for n2 in ("f1h", "f1l", "g1h", "g1l"):
            asb("lw_" + n2, (128, 16 * 128), BF16)
        for n2 in ("f2h", "f2l", "g2h", "g2l"):
            asb("lw_" + n2, (128, HCOLS), BF16)
        asb("lw_bh", (128, 16 * NH), BF16); asb("lw_bl", (128, 16 * NH), BF16)
        asb("a_u32", (128, 1024)); asb("a_uh", (128, 1024), BF16); asb("a_ul", (128, 1024), BF16)
        asb("a_v32", (128, 1024)); asb("a_vh", (128, 1024), BF16); asb("a_vl", (128, 1024), BF16)
        asb("a_t1", (128, 1024)); asb("a_t2", (128, 1024))
        asb("a_brow", (NH, 1024)); asb("a_bcol", (128, NH))
        for p in range(4):
            asb("a_wth%d" % p, (128, 256), BF16); asb("a_wtl%d" % p, (128, 256), BF16)
        asb("a_stg", (128, 1027)); asb("a_cnv1", (128, 1024)); asb("a_cnv2", (128, 1024))
        asb("a_halo", (128, 96))

        e.do("vector", lambda E, S: E.memset(sb["a_halo"][:], 0.0), "ve")
        pb = e.ps
        h = inp["h"]
        a1_sp = {}
        slot_sp = {}
        xn32_guard = None
        for blk in range(8 * NTA):
            r0 = blk * 128
            s4 = blk % 2
            if slot_sp.get(s4) is not None:
                e.need("sync", slot_sp[s4])
            slot_sp[s4], xn32_guard = emit_ln_block(
                e, sb, "a%d_" % s4, (lambda r0=r0: h[r0:r0 + 128, :]),
                (lambda r0=r0: scr["sxnh"][r0:r0 + 128, :]),
                (lambda r0=r0: scr["sxnl"][r0:r0 + 128, :]), guard=xn32_guard,
                ldsem=("hld" if s4 == 0 else "hld1"),
                spsem=("spx" if s4 == 0 else "spx1"))
            if blk % 8 == 7:
                e.need("gpsimd", e.tok("spx"), e.tok("spx1"))
                a1_sp[blk // 8] = e.do("gpsimd", lambda E, S: E.memset(sb["a_mark"][:], 0.0), "gp")
            emit_split_chunk(e, inp, scr, 2)
        emit_split_chunk(e, inp, scr, max(0, 50 - e.split_idx))
        for wn2 in ("wf1", "wg1", "wf2", "wg2", "wb"):
            e.need("sync", *e.t_wsplit[wn2])
        tlow = None
        for dstn, srcn, cols in (("lw_f1h", "h_wf1", 128), ("lw_f1l", "l_wf1", 128),
                                 ("lw_g1h", "h_wg1", 128), ("lw_g1l", "l_wg1", 128),
                                 ("lw_bh", "h_wb", NH), ("lw_bl", "l_wb", NH)):
            tlow = e.dma("sync", "ld", lambda d=dstn: sb[d][:], lambda s=srcn, c=cols: kt3(scr[s], c))
        for dstn, srcn in (("lw_f2h", "h_wf2"), ("lw_f2l", "l_wf2"),
                           ("lw_g2h", "h_wg2"), ("lw_g2l", "l_wg2")):
            tlow = e.dma("sync", "ld", lambda d=dstn: sb[d][:], lambda s=srcn: scr[s].ap())
        emit_split_chunk(e, inp, scr, len(e.split_jobs))

        t_prev_tile = None
        for tt in range(NTA):
            t0 = tt * 1024
            e.need("sync", a1_sp[tt], t_prev_tile)
            for f in range(16):
                e.dma("sync", "xnl", lambda f=f: sb["xnh"][:, f * 1024:(f + 1) * 1024],
                      lambda f=f, t0=t0: scr["sxnh"][t0:t0 + 1024, f * 128:(f + 1) * 128], transpose=True)
                t_xn = e.dma("sync", "xnl", lambda f=f: sb["xnl"][:, f * 1024:(f + 1) * 1024],
                             lambda f=f, t0=t0: scr["sxnl"][t0:t0 + 1024, f * 128:(f + 1) * 128], transpose=True)

            e.need("tensor", t_xn, tlow, e.t_const)
            e.bank("pb0", "pb1")
            for d32, dh, dl, wh, wl in (("a_u32", "a_uh", "a_ul", "lw_f1h", "lw_f1l"),
                                        ("a_v32", "a_vh", "a_vl", "lw_g1h", "lw_g1l")):
                tm = None
                for n in (0, 1):
                    seq = []
                    for kt in range(16):
                        for wt, xt in ((wh, "xnh"), (wh, "xnl"), (wl, "xnh")):
                            seq.append((kt, wt, xt))
                    for j, (kt, wt, xt) in enumerate(seq):
                        tm = e.do("tensor", lambda E, S, n=n, kt=kt, wt=wt, xt=xt, f=(j == 0), l=(j == len(seq) - 1):
                                  E.matmul(pb["pb%d" % n][:, :], sb[wt][:, kt * 128:(kt + 1) * 128],
                                           xslc(sb, xt, kt, n), start=f, stop=l),
                                  "mm" if j == len(seq) - 1 else None)
                e.need("vector", tm)
                for n in (0, 1):
                    e.do("vector", lambda E, S, n=n, d=d32: E.tensor_copy(sb[d][:, n * 512:(n + 1) * 512], pb["pb%d" % n][:, :]), "ve")
                e.need("vector", e.tok("ve"))
                e.do("vector", lambda E, S, dh=dh, d=d32: E.tensor_copy(sb[dh][:], sb[d][:]), "ve")
                e.need("vector", e.tok("ve"))
                tvsplit = e.do("vector", lambda E, S, d=d32, dh=dh, dl=dl: E.tensor_tensor(sb[dl][:], sb[d][:], sb[dh][:], ALU.subtract), "ve")
                e.pbfree["pb0"] = tvsplit
                e.pbfree["pb1"] = tvsplit
                e.bank("pb0", "pb1")

            e.bank("pb2", "pb3")
            last_ts2 = {0: None, 1: None}
            for j in range(8):
                for n in (0, 1):
                    seq = [("a_uh", "lw_f2h"), ("a_ul", "lw_f2h"), ("a_uh", "lw_f2l")]
                    tm = None
                    for pi, (ut, wt) in enumerate(seq):
                        tm = e.do("tensor", lambda E, S, j=j, n=n, ut=ut, wt=wt, f=(pi == 0), l=(pi == 2):
                                  E.matmul(pb["pb%d" % (2 + n)][:, :], sb[ut][:, j * 128:(j + 1) * 128],
                                           sb[wt][:, n * 512:(n + 1) * 512], start=f, stop=l),
                                  "mm" if pi == 2 else None)
                    e.need("scalar", tm, last_ts2[n])
                    ts1 = e.do("scalar", lambda E, S, n=n: E.activation(sb["a_t1"][:, n * 512:(n + 1) * 512], pb["pb%d" % (2 + n)][:, :], AF.Exp, scale=-1.0), "sc")
                    e.need("scalar", ts1)
                    ts2 = e.do("scalar", lambda E, S, n=n: E.activation(sb["a_t2"][:, n * 512:(n + 1) * 512], sb["a_t1"][:, n * 512:(n + 1) * 512], AF.Ln, bias=1.0), "sc")
                    last_ts2[n] = ts2
                    e.pbfree["pb%d" % (2 + n)] = ts1
                    e.bank("pb%d" % (2 + n))
                e.need("gpsimd", ts2)
                tspg = e.dma("gpsimd", "spg", lambda j=j, t0=t0: scr["sg"][t0 + j * 128:t0 + (j + 1) * 128, :], lambda: sb["a_t2"][:])
                e.need("scalar", tspg)
            t_a4_ln = (last_ts2[0], last_ts2[1])
            last_ts3 = {0: t_a4_ln[0], 1: t_a4_ln[1]}
            for j in range(8):
                for n in (0, 1):
                    seq = [("a_vh", "lw_g2h"), ("a_vl", "lw_g2h"), ("a_vh", "lw_g2l")]
                    tm = None
                    for pi, (ut, wt) in enumerate(seq):
                        tm = e.do("tensor", lambda E, S, j=j, n=n, ut=ut, wt=wt, f=(pi == 0), l=(pi == 2):
                                  E.matmul(pb["pb%d" % (2 + n)][:, :], sb[ut][:, j * 128:(j + 1) * 128],
                                           sb[wt][:, n * 512:(n + 1) * 512], start=f, stop=l),
                                  "mm" if pi == 2 else None)
                    e.need("vector", tm, last_ts3[n])
                    tvb = e.do("vector", lambda E, S, n=n: E.tensor_tensor(sb["a_t1"][:, n * 512:(n + 1) * 512], pb["pb%d" % (2 + n)][:, :], e.sb["c_bg2b"][:, n * 512:(n + 1) * 512], ALU.add), "ve")
                    e.need("scalar", tvb)
                    ts3 = e.do("scalar", lambda E, S, n=n: E.activation(sb["a_t2"][:, n * 512:(n + 1) * 512], sb["a_t1"][:, n * 512:(n + 1) * 512], AF.Sigmoid), "sc")
                    last_ts3[n] = ts3
                    e.pbfree["pb%d" % (2 + n)] = tvb
                    e.bank("pb%d" % (2 + n))
                e.need("gpsimd", ts3)
                tspgate = e.dma("gpsimd", "spg", lambda j=j, t0=t0: scr["sgate"][t0 + j * 128:t0 + (j + 1) * 128, :], lambda: sb["a_t2"][:])
                e.need("vector", tspgate)
                e.need("scalar", tspgate)
            e.bank("pb4", "pb5")
            tm = None
            for n in (0, 1):
                seq = []
                for kt in range(16):
                    for wt, xt in (("lw_bh", "xnh"), ("lw_bh", "xnl"), ("lw_bl", "xnh")):
                        seq.append((kt, wt, xt))
                for j, (kt, wt, xt) in enumerate(seq):
                    tm = e.do("tensor", lambda E, S, n=n, kt=kt, wt=wt, xt=xt, f=(j == 0), l=(j == len(seq) - 1):
                              E.matmul(pb["pb%d" % (4 + n)][0:NH, :], sb[wt][:, kt * NH:(kt + 1) * NH],
                                       xslc(sb, xt, kt, n), start=f, stop=l),
                              "mm" if j == len(seq) - 1 else None)
            e.need("scalar", tm)
            e.do("scalar", lambda E, S: E.activation(sb["a_brow"][:, 0:512], pb["pb4"][0:NH, :], AF.Sigmoid), "sc")
            e.need("scalar", e.tok("sc"))
            tsb = e.do("scalar", lambda E, S: E.activation(sb["a_brow"][:, 512:1024], pb["pb5"][0:NH, :], AF.Sigmoid), "sc")
            e.pbfree["pb4"] = tsb
            e.pbfree["pb5"] = tsb
            e.need("tensor", tsb)
            tspb = None
            for j in range(8):
                e.bank("pb6")
                tbt = e.do("tensor", lambda E, S, j=j: E.transpose(pb["pb6"][:, 0:NH], sb["a_brow"][0:NH, j * 128:(j + 1) * 128], e.sb["c_ident"][0:NH, 0:NH]), "mm")
                e.need("vector", tbt)
                if tspb is not None:
                    e.need("vector", tspb)
                tbe = e.do("vector", lambda E, S: E.tensor_copy(sb["a_bcol"][:], pb["pb6"][:, 0:NH]), "ve")
                e.pbfree["pb6"] = tbe
                e.need("gpsimd", tbe)
                tspb = e.dma("gpsimd", "spg", lambda j=j, t0=t0: scr["sbeta"][t0 + j * 128:t0 + (j + 1) * 128, 0:NH], lambda: sb["a_bcol"][:])

            projs = (("q", "wq", "c_cq", "scale"), ("k", "wk", "c_ck", "none"),
                     ("v", "wv", "c_cv", "none"), ("w", "ww", "c_cv", "none"))
            e.bank("pb0", "pb1", "pb2", "pb3")
            last_wmm = {0: None, 1: None, 2: None, 3: None}
            for ip, (pn, wn, cn, post) in enumerate(projs):
                e.need("sync", *e.t_wsplit[wn])
                for og in range(4):
                    qb = 0 if og % 2 == 0 else 4
                    e.bank("pb%d" % qb, "pb%d" % (qb + 1), "pb%d" % (qb + 2), "pb%d" % (qb + 3))
                    tmk = {}
                    for kt in range(16):
                        par = kt % 4
                        e.need("sync", tmk.get(kt - 4, last_wmm[par]))
                        tw = e.dma("sync", "wl%d" % par, lambda par=par: sb["a_wth%d" % par][:, :],
                                   lambda wn=wn, kt=kt, og=og: scr["h_" + wn][kt * 128:(kt + 1) * 128, og * 256:(og + 1) * 256])
                        tw2 = e.dma("sync", "wl%d" % par, lambda par=par: sb["a_wtl%d" % par][:, :],
                                    lambda wn=wn, kt=kt, og=og: scr["l_" + wn][kt * 128:(kt + 1) * 128, og * 256:(og + 1) * 256])
                        e.need("tensor", tw2)
                        tmlast = None
                        for oti in (0, 1):
                            for n in (0, 1):
                                bank = "pb%d" % (qb + oti * 2 + n)
                                for pi, (wt, xt) in enumerate((("a_wth%d" % par, "xnh"), ("a_wth%d" % par, "xnl"), ("a_wtl%d" % par, "xnh"))):
                                    tmlast = e.do("tensor", lambda E, S, wt=wt, xt=xt, kt=kt, oti=oti, n=n, bank=bank, f=(kt == 0 and pi == 0), l=(kt == 15 and pi == 2):
                                                  E.matmul(pb[bank][:, :], sb[wt][:, oti * 128:(oti + 1) * 128],
                                                           xslc(sb, xt, kt, n), start=f, stop=l),
                                                  "mm" if (oti == 1 and n == 1 and pi == 2) else None)
                        tmk[kt] = tmlast
                        last_wmm[par] = tmlast
                    t_projmm = tmk[15]
                    for oti in (0, 1):
                        ot = og * 2 + oti
                        slot = ip * 8 + ot
                        e.need("vector", t_projmm)
                        e.do("vector", lambda E, S, slot=slot: E.tensor_copy(sb["a_stg"][:, 0:3], sb["a_halo"][:, slot * 3:slot * 3 + 3]), "ve")
                        for n in (0, 1):
                            e.need("vector", e.tok("ve"))
                            tev = e.do("vector", lambda E, S, oti=oti, n=n, qb=qb: E.tensor_copy(sb["a_stg"][:, 3 + n * 512:3 + (n + 1) * 512], pb["pb%d" % (qb + oti * 2 + n)][:, :]), "ve")
                            e.pbfree["pb%d" % (qb + oti * 2 + n)] = tev
                        e.need("vector", e.tok("ve"))
                        e.do("vector", lambda E, S, slot=slot: E.tensor_copy(sb["a_halo"][:, slot * 3:slot * 3 + 3], sb["a_stg"][:, 1024:1027]), "ve")
                        e.need("vector", e.tok("ve"))
                        e.do("vector", lambda E, S, cn=cn, ot=ot: E.tensor_scalar(sb["a_cnv1"][:], sb["a_stg"][:, 0:1024], e.sb[cn][:, ot * 4 + 0:ot * 4 + 1], None, ALU.mult), "ve")
                        e.need("vector", e.tok("ve"))
                        e.do("vector", lambda E, S, cn=cn, ot=ot: E.scalar_tensor_tensor(sb["a_cnv2"][:], sb["a_stg"][:, 1:1025], e.sb[cn][:, ot * 4 + 1:ot * 4 + 2], sb["a_cnv1"][:], ALU.mult, ALU.add), "ve")
                        e.need("vector", e.tok("ve"))
                        e.do("vector", lambda E, S, cn=cn, ot=ot: E.scalar_tensor_tensor(sb["a_cnv1"][:], sb["a_stg"][:, 2:1026], e.sb[cn][:, ot * 4 + 2:ot * 4 + 3], sb["a_cnv2"][:], ALU.mult, ALU.add), "ve")
                        e.need("vector", e.tok("ve"))
                        tv2 = e.do("vector", lambda E, S, cn=cn, ot=ot: E.scalar_tensor_tensor(sb["a_cnv2"][:], sb["a_stg"][:, 3:1027], e.sb[cn][:, ot * 4 + 3:ot * 4 + 4], sb["a_cnv1"][:], ALU.mult, ALU.add), "ve")
                        e.need("scalar", tv2)
                        tsil = e.do("scalar", lambda E, S: E.activation(sb["a_cnv1"][:], sb["a_cnv2"][:], AF.Silu), "sc")
                        if post == "scale":
                            e.need("vector", tsil)
                            tfin = e.do("vector", lambda E, S: E.tensor_scalar(sb["a_cnv2"][:], sb["a_cnv1"][:], SCALE, None, ALU.mult), "ve")
                            src = "a_cnv2"
                        else:
                            tfin = tsil
                            src = "a_cnv1"
                        e.need("gpsimd", tfin)
                        tspill = e.dma("gpsimd", "sp", lambda pn=pn, ot=ot, t0=t0: scr["s" + pn][ot * 128:(ot + 1) * 128, t0:t0 + 1024],
                                       lambda src=src: sb[src][:])
                        e.need("vector", tspill)
                        e.need("scalar", tspill)
                        e.pbfree["pb%d" % (qb + oti * 2)] = tspill
                        e.pbfree["pb%d" % (qb + oti * 2 + 1)] = tspill
            t_prev_tile = e.tok("mm")
            emit_quant_chunk(e, inp, scr, (len(e.quant_jobs) + NTA - 1) // NTA)
        emit_quant_chunk(e, inp, scr, len(e.quant_jobs))
        e.t_projspill = e.tok("sp")
        e.t_projspill2 = e.tok("spg")
        e.t_a1_spx = a1_sp[NTA - 1]


def emit_phaseB(e, inp, scr, nc):
    T = e.T
    NTS = T // 128
    with contextlib.ExitStack() as bes:
        sb = {}

        def bsb(name, shape, dt=F32):
            sb[name] = bes.enter_context(nc.sbuf_tensor(name, list(shape), dt))

        for g in (0, 1):
            for p in (0, 1):
                for nm in ("q", "kf", "vf", "wf", "g", "sgt"):
                    bsb(f"L{g}{p}{nm}", (128, 512))
                bsb(f"L{g}{p}bet", (128, 4))
            for nm in ("ktm", "vtm", "wtm", "expg", "expmg", "v2", "tmp", "pm",
                       "sexp", "a2", "a2t", "v2f", "w2m", "o1", "o2"):
                bsb(f"B{g}{nm}", (128, 512))
            bsb(f"B{g}gcc", (128, 16)); bsb(f"B{g}egc", (1, 512))
            for nm in ("wn", "smax", "ssum", "srec", "orms"):
                bsb(f"B{g}{nm}", (128, 8))
            for i in (0, 1):
                bsb(f"SK{g}_{i}", (128, 512))
            for i in range(5):
                bsb(f"SV{g}_{i}", (128, 512))

        pb = e.ps
        tz = None
        for g in (0, 1):
            e.do("vector", lambda E, S, g=g: E.memset(sb[f"SK{g}_0"][:], 0.0), "ve")
            tz = e.do("vector", lambda E, S, g=g: E.memset(sb[f"SV{g}_0"][:], 0.0), "ve")
        e.need("tensor", tz)

        sk_par = {0: 0, 1: 0}
        state_tok = {0: None, 1: None}
        sv_tok = {0: None, 1: None}
        o_read_tok = {}
        prev_spill = {0: None, 1: None}
        last_spill = [None]
        prev_loads = {0: {}, 1: {}}

        e.need("tensor", e.t_projspill, e.t_projspill2)
        qtoks = [t for t in (e.quant_prev[0], e.quant_prev[1]) if t is not None]
        e.need("sync", e.t_projspill, e.t_projspill2, e.tok("ve"), e.tok("sc"), e.tok("gp"), *qtoks)
        e.bank(*["pb%d" % i for i in range(8)])
        mask3 = lambda: bass.AP(e.sb["c_mask"], 0, [[128, 128], [0, 4], [1, 128]])

        for ts in range(NTS):
            t0 = ts * 128
            for g in (0, 1):
                p = ts % 2
                L = {nm: f"L{g}{p}{nm}" for nm in ("q", "kf", "vf", "wf", "g", "sgt")}

                def B(nm, g=g):
                    return f"B{g}{nm}"

                sbank = "pb%d" % (4 + g)
                ld = {}
                blsem = f"bl{g}{p}"
                pl = prev_loads[g].get(p)
                if pl is not None:
                    e.need("sync", pl)
                for nm, dram in (("q", "sq"), ("kf", "sk"), ("vf", "sv"), ("wf", "sw")):
                    ld[nm] = e.dma("sync", blsem, lambda g=g, p=p, nm=nm: sb[f"L{g}{p}{nm}"][:],
                                   lambda dram=dram, g=g, t0=t0: tile3(scr[dram], g * 512, t0, 4, 128, T))
                for nm, dram in (("g", "sg"), ("sgt", "sgate")):
                    ld[nm] = e.dma("sync", blsem, lambda g=g, p=p, nm=nm: sb[f"L{g}{p}{nm}"][:],
                                   lambda dram=dram, g=g, t0=t0: bass.AP(scr[dram], t0 * HCOLS + g * 512, [[HCOLS, 128], [1, 512]]))
                ld["bet"] = e.dma("sync", blsem, lambda g=g, p=p: sb[f"L{g}{p}bet"][:],
                                  lambda g=g, t0=t0: bass.AP(scr["sbeta"], t0 * NH + g * 4, [[NH, 128], [1, 4]]))
                for nm in ld:
                    ld[nm] = ld["bet"]

                e.need("tensor", ld["kf"], ld["vf"], ld["wf"])
                e.bank("pb0", "pb1", "pb2")
                tt1 = tt2 = tt3 = None
                for hh in range(4):
                    tt1 = e.do("tensor", lambda E, S, hh=hh, L=L, B=B, sbank=sbank, g=g, p=p: E.transpose(pb["pb0"][:, hh * 128:(hh + 1) * 128], sb[L["wf"]][:, hh * 128:(hh + 1) * 128], e.sb["c_ident"][:]), "mm")
                for hh in range(4):
                    tt2 = e.do("tensor", lambda E, S, hh=hh, L=L, B=B, sbank=sbank, g=g, p=p: E.transpose(pb["pb1"][:, hh * 128:(hh + 1) * 128], sb[L["kf"]][:, hh * 128:(hh + 1) * 128], e.sb["c_ident"][:]), "mm")
                for hh in range(4):
                    tt3 = e.do("tensor", lambda E, S, hh=hh, L=L, B=B, sbank=sbank, g=g, p=p: E.transpose(pb["pb2"][:, hh * 128:(hh + 1) * 128], sb[L["vf"]][:, hh * 128:(hh + 1) * 128], e.sb["c_ident"][:]), "mm")
                e.need("scalar", tt1)
                ev1 = e.do("scalar", lambda E, S, B=B, sbank=sbank, g=g, p=p: E.copy(sb[B("wtm")][:], pb["pb0"][:, :]), "sc")
                e.need("scalar", tt2)
                ev2 = e.do("scalar", lambda E, S, B=B, sbank=sbank, g=g, p=p: E.copy(sb[B("ktm")][:], pb["pb1"][:, :]), "sc")
                e.need("scalar", tt3)
                ev3 = e.do("scalar", lambda E, S, B=B, sbank=sbank, g=g, p=p: E.copy(sb[B("vtm")][:], pb["pb2"][:, :]), "sc")
                e.pbfree["pb0"] = ev1

                e.need("tensor", ld["g"], ev2)
                tG = e.do("tensor", lambda E, S, L=L, B=B, sbank=sbank, g=g, p=p: E.matmul(pb["pb1"][:, :], e.sb["c_trineg"][:, :], sb[L["g"]][:], start=True, stop=True), "mm")
                e.need("scalar", tG)
                tEg = e.do("scalar", lambda E, S, B=B, sbank=sbank, g=g, p=p: E.activation(sb[B("expg")][:], pb["pb1"][:, :], AF.Exp), "sc")
                e.need("scalar", tEg)
                tEmg = e.do("scalar", lambda E, S, B=B, sbank=sbank, g=g, p=p: E.activation(sb[B("expmg")][:], pb["pb1"][:, :], AF.Exp, scale=-1.0), "sc")
                e.pbfree["pb1"] = tEmg
                e.need("tensor", ev3)
                tGC = None
                for cc in range(4):
                    for hh in range(4):
                        tp = (96, 0) if cc == 3 else None
                        tGC = e.do("tensor", lambda E, S, cc=cc, hh=hh, L=L, tp=tp, B=B, sbank=sbank, g=g, p=p: E.matmul(pb["pb2"][:, cc * 4 + hh:cc * 4 + hh + 1],
                                                                                     sb[L["g"]][cc * 32:(cc + 1) * 32, hh * 128:(hh + 1) * 128],
                                                                                     e.sb["c_negcol"][cc * 32:(cc + 1) * 32, :], start=(cc == 0 and hh == 0), stop=(cc == 3 and hh == 3), tile_position=tp), "mm")
                e.need("scalar", tGC)
                tGCe = e.do("scalar", lambda E, S, B=B, sbank=sbank, g=g, p=p: E.activation(sb[B("gcc")][:], pb["pb2"][:, 0:16], AF.Exp), "sc")
                e.pbfree["pb2"] = tGCe

                e.need("scalar", ev1)
                tsq = e.do("scalar", lambda E, S, B=B, sbank=sbank, g=g, p=p: E.activation(sb[B("tmp")][:], sb[B("wtm")][:], AF.Square), "sc")
                e.need("vector", tsq)
                tr = e.do("vector", lambda E, S, B=B, sbank=sbank, g=g, p=p: E.tensor_reduce(red3(sb[B("wn")]), t3(sb[B("tmp")]), AX.X, ALU.add), "ve")
                e.need("scalar", tr)
                tss = None
                for c4 in range(4):
                    tss = e.do("scalar", lambda E, S, c4=c4, B=B, sbank=sbank, g=g, p=p: E.activation(sb[B("wn")][:, 4 + c4:5 + c4], sb[B("wn")][:, c4:c4 + 1], AF.Sqrt, bias=EPS_L2), "sc")
                e.need("vector", tss)
                trc = e.do("vector", lambda E, S, B=B, sbank=sbank, g=g, p=p: E.reciprocal(sb[B("wn")][:, 0:4], sb[B("wn")][:, 4:8]), "ve")
                e.need("vector", trc, tEmg)
                e.do("vector", lambda E, S, B=B, sbank=sbank, g=g, p=p: E.tensor_tensor(sb[B("tmp")][:], sb[B("wtm")][:], sb[B("expmg")][:], ALU.mult), "ve")
                e.need("vector", e.tok("ve"))
                e.do("vector", lambda E, S, B=B, sbank=sbank, g=g, p=p: E.tensor_tensor(t3(sb[B("sexp")]), t3(sb[B("tmp")]), bc3(sb[B("wn")]), ALU.mult), "ve")
                e.need("vector", e.tok("ve"), ld["bet"])
                tV2 = e.do("vector", lambda E, S, L=L, B=B, sbank=sbank, g=g, p=p: E.tensor_tensor(t3(sb[B("v2")]), t3(sb[B("sexp")]), bc3(sb[f"L{g}{p}bet"]), ALU.mult), "ve")

                e.need("tensor", ld["q"])
                e.bank("pb3")
                tP = None
                for hh in range(4):
                    tP = e.do("tensor", lambda E, S, hh=hh, L=L, B=B, sbank=sbank, g=g, p=p: E.matmul(pb["pb3"][:, hh * 128:(hh + 1) * 128],
                                                                         sb[L["kf"]][:, hh * 128:(hh + 1) * 128],
                                                                         sb[L["q"]][:, hh * 128:(hh + 1) * 128],
                                                                         start=(hh == 0), stop=(hh == 3)), "mm")
                e.need("vector", tP)
                tPM = e.do("vector", lambda E, S, B=B, sbank=sbank, g=g, p=p: E.tensor_tensor(t3(sb[B("pm")]), t3b(pb["pb3"]), mask3(), ALU.mult), "ve")
                e.pbfree["pb3"] = tPM

                e.bank(sbank)
                scores_last = None
                for cc in range(4):
                    gci = ts * 4 + cc
                    skc = f"SK{g}_{sk_par[g]}"
                    skn = f"SK{g}_{1 - sk_par[g]}"
                    svc = f"SV{g}_{gci % 5}"
                    svn = f"SV{g}_{(gci + 1) % 5}"
                    e.need("tensor", state_tok[g], tV2, tPM)
                    for hh in range(4):
                        tp = (0, 96) if cc == 3 else None
                        e.do("tensor", lambda E, S, hh=hh, cc=cc, skc=skc, tp=tp, L=L, B=B, sbank=sbank, g=g, p=p:
                             E.matmul(pb[sbank][cc * 32:(cc + 1) * 32, hh * 128:(hh + 1) * 128],
                                      sb[L["q"]][:, hh * 128 + cc * 32:hh * 128 + (cc + 1) * 32],
                                      sb[skc][:, hh * 128:(hh + 1) * 128],
                                      start=(hh == 0), stop=False, tile_position=tp), "mm")
                    for hh in range(4):
                        tp = (96, 96) if cc == 3 else None
                        scores_last = e.do("tensor", lambda E, S, hh=hh, cc=cc, tp=tp, B=B, sbank=sbank, g=g, p=p:
                                           E.matmul(pb[sbank][cc * 32:(cc + 1) * 32, hh * 128:(hh + 1) * 128],
                                                    sb[B("pm")][cc * 32:(cc + 1) * 32, hh * 128 + cc * 32:hh * 128 + (cc + 1) * 32],
                                                    sb[B("v2")][cc * 32:(cc + 1) * 32, hh * 128:(hh + 1) * 128],
                                                    start=False, stop=(hh == 3), tile_position=tp), "mm")
                    e.bank("pb6", "pb7")
                    tdk = None
                    tpr = (96, 0) if cc == 3 else None
                    for hh in range(4):
                        tdk = e.do("tensor", lambda E, S, hh=hh, cc=cc, tpr=tpr, B=B, sbank=sbank, g=g, p=p:
                                   E.matmul(pb["pb6"][:, hh * 128:(hh + 1) * 128],
                                            sb[B("ktm")][cc * 32:(cc + 1) * 32, hh * 128:(hh + 1) * 128],
                                            sb[B("v2")][cc * 32:(cc + 1) * 32, hh * 128:(hh + 1) * 128],
                                            start=(hh == 0), stop=(hh == 3), tile_position=tpr), "mm")
                    tdv = None
                    for hh in range(4):
                        tdv = e.do("tensor", lambda E, S, hh=hh, cc=cc, tpr=tpr, B=B, sbank=sbank, g=g, p=p:
                                   E.matmul(pb["pb7"][:, hh * 128:(hh + 1) * 128],
                                            sb[B("v2")][cc * 32:(cc + 1) * 32, hh * 128:(hh + 1) * 128],
                                            sb[B("vtm")][cc * 32:(cc + 1) * 32, hh * 128:(hh + 1) * 128],
                                            start=(hh == 0), stop=(hh == 3), tile_position=tpr), "mm")
                    e.bank("pb1")
                    tpr1 = (96, 0) if cc == 3 else None
                    tgr = e.do("tensor", lambda E, S, cc=cc, L=L, tpr1=tpr1, B=B, sbank=sbank, g=g, p=p: E.matmul(pb["pb1"][0:1, :], e.sb["c_negcol"][cc * 32:(cc + 1) * 32, :],
                                                                                     sb[L["g"]][cc * 32:(cc + 1) * 32, :], start=True, stop=True, tile_position=tpr1), "mm")
                    e.need("scalar", tgr)
                    tegc = e.do("scalar", lambda E, S, B=B, sbank=sbank, g=g, p=p: E.activation(sb[B("egc")][:], pb["pb1"][0:1, :], AF.Exp), "sc")
                    e.need("tensor", tegc)
                    tD = e.do("tensor", lambda E, S, B=B, sbank=sbank, g=g, p=p: E.matmul(pb["pb1"][:, :], e.sb["c_onesrow"][:, :],
                                                             sb[B("egc")][:], start=True, stop=True), "mm")
                    e.need("vector", tD, tdk)
                    e.do("vector", lambda E, S, skc=skc, skn=skn, B=B, sbank=sbank, g=g, p=p: E.tensor_tensor(sb[skn][:], sb[skc][:], pb["pb6"][:, :], ALU.add), "ve")
                    e.need("vector", e.tok("ve"))
                    tSk = e.do("vector", lambda E, S, skn=skn, B=B, sbank=sbank, g=g, p=p: E.tensor_tensor(sb[skn][:], sb[skn][:], pb["pb1"][:, :], ALU.mult), "ve")
                    e.pbfree["pb1"] = tSk
                    e.pbfree["pb6"] = tSk
                    otok = o_read_tok.get((g, gci - 4))
                    e.need("vector", tdv, otok, sv_tok[g])
                    tSva = e.do("vector", lambda E, S, svc=svc, svn=svn, B=B, sbank=sbank, g=g, p=p: E.tensor_tensor(sb[svn][:], sb[svc][:], pb["pb7"][:, :], ALU.add), "ve")
                    e.need("gpsimd", tSva, tGCe)
                    tSv = None
                    for hh in range(4):
                        e.need("gpsimd", e.tok("gp"))
                        tSv = e.do("gpsimd", lambda E, S, svn=svn, cc=cc, hh=hh, B=B, sbank=sbank, g=g, p=p: E.tensor_scalar(sb[svn][:, hh * 128:(hh + 1) * 128], sb[svn][:, hh * 128:(hh + 1) * 128],
                                                                                                sb[B("gcc")][:, cc * 4 + hh:cc * 4 + hh + 1], None, ALU.mult), "gp")
                    e.pbfree["pb7"] = tSva
                    sv_tok[g] = tSv
                    state_tok[g] = tSk
                    sk_par[g] = 1 - sk_par[g]

                e.need("vector", scores_last, tEg)
                tsub = e.do("vector", lambda E, S, B=B, sbank=sbank, g=g, p=p: E.tensor_tensor(sb[B("tmp")][:], pb[sbank][:, :], sb[B("expg")][:], ALU.mult), "ve")
                e.need("vector", tsub)
                tmx = e.do("vector", lambda E, S, B=B, sbank=sbank, g=g, p=p: E.tensor_reduce(red3(sb[B("smax")]), t3(sb[B("tmp")]), AX.X, ALU.max), "ve")
                e.need("vector", tmx)
                tsub = e.do("vector", lambda E, S, B=B, sbank=sbank, g=g, p=p: E.tensor_tensor(t3(sb[B("o1")]), t3(sb[B("tmp")]), bc3(sb[B("smax")]), ALU.subtract), "ve")
                e.need("scalar", tsub)
                tex = e.do("scalar", lambda E, S, B=B, sbank=sbank, g=g, p=p: E.activation(sb[B("sexp")][:], sb[B("o1")][:], AF.Exp), "sc")
                e.need("vector", tex)
                tsum = e.do("vector", lambda E, S, B=B, sbank=sbank, g=g, p=p: E.tensor_reduce(red3(sb[B("ssum")]), t3(sb[B("sexp")]), AX.X, ALU.add), "ve")
                e.need("vector", tsum)
                trec = e.do("vector", lambda E, S, B=B, sbank=sbank, g=g, p=p: E.reciprocal(sb[B("srec")][:, 0:4], sb[B("ssum")][:, 0:4]), "ve")
                e.need("vector", trec, tEg)
                e.do("vector", lambda E, S, B=B, sbank=sbank, g=g, p=p: E.tensor_tensor(sb[B("tmp")][:], sb[B("sexp")][:], sb[B("expg")][:], ALU.mult), "ve")
                e.need("vector", e.tok("ve"))
                tA2 = e.do("vector", lambda E, S, B=B, sbank=sbank, g=g, p=p: E.tensor_tensor(t3(sb[B("a2")]), t3(sb[B("tmp")]), bc3(sb[B("srec")]), ALU.mult), "ve")

                e.need("tensor", tA2)
                e.bank("pb0")
                tat = None
                for hh in range(4):
                    tat = e.do("tensor", lambda E, S, hh=hh, B=B, sbank=sbank, g=g, p=p: E.transpose(pb["pb0"][:, hh * 128:(hh + 1) * 128], sb[B("a2")][:, hh * 128:(hh + 1) * 128], e.sb["c_ident"][:]), "mm")
                e.need("scalar", tat)
                evat = e.do("scalar", lambda E, S, B=B, sbank=sbank, g=g, p=p: E.copy(sb[B("a2t")][:], pb["pb0"][:, :]), "sc")
                e.need("tensor", evat)
                tvf = None
                for hh in range(4):
                    tvf = e.do("tensor", lambda E, S, hh=hh, B=B, sbank=sbank, g=g, p=p: E.transpose(pb["pb0"][:, hh * 128:(hh + 1) * 128], sb[B("v2")][:, hh * 128:(hh + 1) * 128], e.sb["c_ident"][:]), "mm")
                e.need("scalar", tvf)
                evvf = e.do("scalar", lambda E, S, B=B, sbank=sbank, g=g, p=p: E.copy(sb[B("v2f")][:], pb["pb0"][:, :]), "sc")
                e.pbfree["pb0"] = evvf

                e.need("tensor", evvf)
                e.bank("pb2")
                tw2 = None
                for hh in range(4):
                    tw2 = e.do("tensor", lambda E, S, hh=hh, B=B, sbank=sbank, g=g, p=p: E.matmul(pb["pb2"][:, hh * 128:(hh + 1) * 128],
                                                                     sb[B("v2f")][:, hh * 128:(hh + 1) * 128],
                                                                     sb[B("a2t")][:, hh * 128:(hh + 1) * 128],
                                                                     start=(hh == 0), stop=(hh == 3)), "mm")
                e.need("vector", tw2)
                tw2m = e.do("vector", lambda E, S, B=B, sbank=sbank, g=g, p=p: E.tensor_tensor(t3(sb[B("w2m")]), t3b(pb["pb2"]), mask3(), ALU.mult), "ve")
                e.pbfree["pb2"] = tw2m

                e.need("tensor", tsub, tw2m, sv_tok[g])
                last_o = None
                for cc in range(4):
                    gci = ts * 4 + cc
                    svc = f"SV{g}_{gci % 5}"
                    for hh in range(4):
                        tp = (0, 96) if cc == 3 else None
                        e.do("tensor", lambda E, S, hh=hh, cc=cc, svc=svc, tp=tp, B=B, sbank=sbank, g=g, p=p:
                             E.matmul(pb[sbank][cc * 32:(cc + 1) * 32, hh * 128:(hh + 1) * 128],
                                      sb[B("a2t")][:, hh * 128 + cc * 32:hh * 128 + (cc + 1) * 32],
                                      sb[svc][:, hh * 128:(hh + 1) * 128],
                                      start=(hh == 0), stop=False, tile_position=tp), "mm")
                    for hh in range(4):
                        tp = (96, 96) if cc == 3 else None
                        last_o = e.do("tensor", lambda E, S, hh=hh, cc=cc, tp=tp, B=B, sbank=sbank, g=g, p=p:
                                      E.matmul(pb[sbank][cc * 32:(cc + 1) * 32, hh * 128:(hh + 1) * 128],
                                               sb[B("w2m")][cc * 32:(cc + 1) * 32, hh * 128 + cc * 32:hh * 128 + (cc + 1) * 32],
                                               sb[B("vtm")][cc * 32:(cc + 1) * 32, hh * 128:(hh + 1) * 128],
                                               start=False, stop=(hh == 3), tile_position=tp), "mm")
                for cc in range(4):
                    o_read_tok[(g, ts * 4 + cc)] = last_o

                e.need("scalar", last_o)
                tosq = e.do("scalar", lambda E, S, B=B, sbank=sbank, g=g, p=p: E.activation(sb[B("tmp")][:], pb[sbank][:, :], AF.Square), "sc")
                e.need("vector", tosq)
                tors = e.do("vector", lambda E, S, B=B, sbank=sbank, g=g, p=p: E.tensor_reduce(red3(sb[B("orms")]), t3(sb[B("tmp")]), AX.X, ALU.add), "ve")
                e.need("vector", tors)
                tsc = e.do("vector", lambda E, S, B=B, sbank=sbank, g=g, p=p: E.tensor_scalar(sb[B("orms")][:, 4:8], sb[B("orms")][:, 0:4], 1.0 / HD, None, ALU.mult), "ve")
                e.need("scalar", tsc)
                tsq3 = None
                for c4 in range(4):
                    tsq3 = e.do("scalar", lambda E, S, c4=c4, B=B, sbank=sbank, g=g, p=p: E.activation(sb[B("orms")][:, c4:c4 + 1], sb[B("orms")][:, 4 + c4:5 + c4], AF.Sqrt, bias=EPS_RMS), "sc")
                e.need("vector", tsq3)
                torc = e.do("vector", lambda E, S, B=B, sbank=sbank, g=g, p=p: E.reciprocal(sb[B("orms")][:, 4:8], sb[B("orms")][:, 0:4]), "ve")
                e.need("vector", torc, ld["sgt"], last_spill[0])
                e.do("vector", lambda E, S, L=L, B=B, sbank=sbank, g=g, p=p: E.tensor_tensor(sb[B("o1")][:], pb[sbank][:, :], sb[L["sgt"]][:], ALU.mult), "ve")
                e.need("vector", e.tok("ve"))
                e.do("vector", lambda E, S, B=B, sbank=sbank, g=g, p=p: E.tensor_tensor(t3(sb[B("o2")]), t3(sb[B("o1")]), bc3(sb[B("orms")], 4), ALU.mult), "ve")
                e.need("vector", e.tok("ve"))
                tofin = e.do("vector", lambda E, S, B=B, sbank=sbank, g=g, p=p: E.tensor_tensor(sb[B("o1")][:], sb[B("o2")][:], e.sb["c_normwb"][:], ALU.mult), "ve")
                e.pbfree[sbank] = tofin

                e.need("tensor", tofin)
                e.bank("pb0")
                tot = None
                for hh in range(4):
                    tot = e.do("tensor", lambda E, S, hh=hh, B=B, sbank=sbank, g=g, p=p: E.transpose(pb["pb0"][:, hh * 128:(hh + 1) * 128], sb[B("o1")][:, hh * 128:(hh + 1) * 128], e.sb["c_ident"][:]), "mm")
                e.need("vector", tot, last_spill[0])
                evo = e.do("vector", lambda E, S, B=B, sbank=sbank, g=g, p=p: E.tensor_copy(sb[B("o2")][:], pb["pb0"][:, :]), "ve")
                e.pbfree["pb0"] = evo
                e.need("gpsimd", evo)
                prev_spill[g] = e.dma("gpsimd", "sp", lambda g=g, t0=t0: tile3(scr["cc_in"], g * 512, t0, 4, 128, T),
                                      lambda B=B: sb[B("o2")][:])
                last_spill[0] = prev_spill[g]
                prev_loads[g] = dict(prev_loads[g])
                prev_loads[g][p] = tofin
        e.t_scan_spill = e.tok("sp")
        e.t_scan_ve = e.tok("ve")
        e.t_scan_gp = e.tok("gp")
        e.t_scan_sc = e.tok("sc")


def emit_collective(e, scr):
    e.need("gpsimd", e.t_scan_spill)
    e.t_cc = e.do("gpsimd", lambda E, S: E.collective_compute(
        "AllGather", ALU.bypass,
        replica_groups=[[0, 1], [2, 3], [4, 5], [6, 7]],
        ins=[scr["cc_in"].ap().opt()],
        outs=[scr["cc_out"].ap().opt()]), "cc")


def emit_ffn(e, inp, scr, nc):
    T, TO = e.T, e.TO
    with contextlib.ExitStack() as fes:
        sb = {}

        def fsb(name, shape, dt=F32):
            sb[name] = fes.enter_context(nc.sbuf_tensor(name, list(shape), dt))

        e.need("sync", e.t_a1_spx, e.t_const)
        e.op("sync", lambda E, S: E.reg_load(e.r_tmp, e.sb["c_offs"][0:1, 1:2]))
        for nm_s, nm_d in (("sxnh", "xnoh"), ("sxnl", "xnol")):
            prev_sp = e.do("sync", lambda E, S, nm_s=nm_s, nm_d=nm_d, TO=TO: E.dma_start(
                out=bass.AP(scr[nm_d], 0, [[HID, 128], [HID * 128, TO // 128], [1, HID]]),
                in_=bass.AP(scr[nm_s], e.r_tmp, [[HID, 128], [HID * 128, TO // 128], [1, HID]])), "odl", 16)
        e.need("sync", e.t_scan_spill, e.t_scan_ve, e.t_scan_gp, e.t_scan_sc, prev_sp)

        pb = e.ps
        e.need("sync", e.quant_prev[0], e.quant_prev[1])
        e.need("tensor", e.t_scan_ve, e.t_scan_gp, e.t_scan_sc)
        e.bank(*["pb%d" % i for i in range(8)])
        f1es = contextlib.ExitStack()

        def f1sb(name, shape, dt=F32):
            sb[name] = f1es.enter_context(nc.sbuf_tensor(name, list(shape), dt))

        f1sb("x2h", (128, 16 * 1024), BF16); f1sb("x2l", (128, 16 * 1024), BF16)
        for p2 in range(4):
            f1sb("f_wg%d" % p2, (128, HID), BF16)
            f1sb("f_wu%d" % p2, (128, HID), BF16)
        f1sb("f_sil", (128, 1024)); f1sb("f_fu32", (128, 1024))
        f1sb("f_fh", (128, 1024), BF16); f1sb("f_fl", (128, 1024), BF16)
        last_spt = None
        for tc2 in range(TO // 1024):
            c0 = tc2 * 1024
            t_x = None
            if last_spt is not None:
                e.need("sync", last_spt)
            for f in range(16):
                e.dma("sync", "fxl", lambda f=f: sb["x2h"][:, f * 1024:(f + 1) * 1024],
                      lambda f=f, c0=c0: scr["xnoh"][c0:c0 + 1024, f * 128:(f + 1) * 128], transpose=True)
                t_x = e.dma("sync", "fxl", lambda f=f: sb["x2l"][:, f * 1024:(f + 1) * 1024],
                            lambda f=f, c0=c0: scr["xnol"][c0:c0 + 1024, f * 128:(f + 1) * 128], transpose=True)
            e.need("tensor", t_x)
            spt = {}
            tmu_hist = {}
            for it in range(FFI // 128):
                p2 = it % 4
                if tmu_hist.get(it - 4) is not None:
                    e.need("sync", tmu_hist[it - 4])
                tg = e.dma("sync", "fw%d" % p2, lambda p2=p2: sb["f_wg%d" % p2][:, :],
                           lambda it=it: bass.AP(scr["q_gw"], it * 128, [[FFI, 128], [FFI * 128, 16], [1, 128]]))
                tu = e.dma("sync", "fw%d" % p2, lambda p2=p2: sb["f_wu%d" % p2][:, :],
                           lambda it=it: bass.AP(scr["q_uw"], it * 128, [[FFI, 128], [FFI * 128, 16], [1, 128]]))
                e.need("tensor", tu)
                tmg = tmu = None
                for n in (0, 1):
                    for kt in range(16):
                        for xt in ("x2h", "x2l"):
                            tmg = e.do("tensor", lambda E, S, p2=p2, n=n, kt=kt, xt=xt, f=(kt == 0 and xt == "x2h"), l=(kt == 15 and xt == "x2l"):
                                       E.matmul(pb["pb%d" % n][:, :], sb["f_wg%d" % p2][:, kt * 128:(kt + 1) * 128],
                                                xslc(sb, xt, kt, n), start=f, stop=l),
                                       "mm" if (kt == 15 and xt == "x2l") else None)
                for n in (0, 1):
                    for kt in range(16):
                        for xt in ("x2h", "x2l"):
                            tmu = e.do("tensor", lambda E, S, p2=p2, n=n, kt=kt, xt=xt, f=(kt == 0 and xt == "x2h"), l=(kt == 15 and xt == "x2l"):
                                       E.matmul(pb["pb%d" % (2 + n)][:, :], sb["f_wu%d" % p2][:, kt * 128:(kt + 1) * 128],
                                                xslc(sb, xt, kt, n), start=f, stop=l),
                                       "mm" if (kt == 15 and xt == "x2l") else None)
                tmu_hist[it] = tmu
                e.need("scalar", tmg)
                tsg = None
                for n in (0, 1):
                    tsg = e.do("scalar", lambda E, S, n=n, it=it: E.activation(sb["f_sil"][:, n * 512:(n + 1) * 512], pb["pb%d" % n][:, :], AF.Silu,
                                                                              scale=e.sb["c_gsc"][:, it:it + 1]), "sc")
                e.need("vector", tsg, tmu)
                tfu = None
                for n in (0, 1):
                    tfu = e.do("vector", lambda E, S, n=n, it=it: E.scalar_tensor_tensor(sb["f_fu32"][:, n * 512:(n + 1) * 512], sb["f_sil"][:, n * 512:(n + 1) * 512],
                                                                                        e.sb["c_usc"][:, it:it + 1], pb["pb%d" % (2 + n)][:, :], ALU.mult, ALU.mult), "ve")
                for i2 in range(4):
                    e.pbfree["pb%d" % i2] = tfu
                e.bank("pb0", "pb1", "pb2", "pb3")
                e.need("vector", tfu)
                e.do("vector", lambda E, S: E.tensor_copy(sb["f_fh"][:], sb["f_fu32"][:]), "ve")
                e.need("vector", e.tok("ve"))
                tfl = e.do("vector", lambda E, S: E.tensor_tensor(sb["f_fl"][:], sb["f_fu32"][:], sb["f_fh"][:], ALU.subtract), "ve")
                e.need("gpsimd", tfl)
                e.dma("gpsimd", "sp", lambda it=it, c0=c0: scr["fuh"][it * 128:(it + 1) * 128, c0:c0 + 1024], lambda: sb["f_fh"][:])
                spt[it] = e.dma("gpsimd", "sp", lambda it=it, c0=c0: scr["ful"][it * 128:(it + 1) * 128, c0:c0 + 1024], lambda: sb["f_fl"][:])
                e.need("vector", spt[it])
                e.need("scalar", spt[it])
                last_spt = spt[it]
        t_fuse = e.tok("sp")
        f1es.close()

        fsb("d_fh", (128, 48 * 512), BF16)
        fsb("d_fl", (128, 48 * 512), BF16)
        for p2 in range(4):
            fsb("d_w%d" % p2, (128, 48 * 128), BF16)
        fsb("d_out", (128, 512))
        last_tvd = None
        for tc4 in range(TO // 512):
            c0 = tc4 * 512
            e.need("sync", t_fuse)
            if last_tvd is not None:
                e.need("sync", last_tvd)
            tfh = e.dma("sync", "fcl", lambda: sb["d_fh"][:, :],
                        lambda c0=c0: bass.AP(scr["fuh"], c0, [[TO, 128], [TO * 128, 48], [1, 512]]))
            tfl2 = e.dma("sync", "fcl", lambda: sb["d_fl"][:, :],
                         lambda c0=c0: bass.AP(scr["ful"], c0, [[TO, 128], [TO * 128, 48], [1, 512]]))
            dsp = {}
            tmd_hist = {}
            for ot in range(16):
                p2 = ot % 4
                if tmd_hist.get(ot - 4) is not None:
                    e.need("sync", tmd_hist[ot - 4])
                tw = e.dma("sync", "dw%d" % p2, lambda p2=p2: sb["d_w%d" % p2][:, :],
                           lambda ot=ot: bass.AP(scr["q_dw"], ot * 128, [[HID, 128], [HID * 128, 48], [1, 128]]))
                e.need("tensor", tw, tfl2)
                e.bank("pb%d" % (4 + p2))
                tmd = None
                for kt in range(48):
                    for ft in ("d_fh", "d_fl"):
                        tmd = e.do("tensor", lambda E, S, p2=p2, kt=kt, ft=ft, f=(kt == 0 and ft == "d_fh"), l=(kt == 47 and ft == "d_fl"):
                                   E.matmul(pb["pb%d" % (4 + p2)][:, :], sb["d_w%d" % p2][:, kt * 128:(kt + 1) * 128],
                                            sb[ft][:, kt * 512:(kt + 1) * 512], start=f, stop=l),
                                   "mm" if (kt == 47 and ft == "d_fl") else None)
                tmd_hist[ot] = tmd
                e.need("vector", tmd)
                tvd = e.do("vector", lambda E, S, ot=ot, p2=p2: E.tensor_scalar(sb["d_out"][:], pb["pb%d" % (4 + p2)][:, :], e.sb["c_dsc"][:, ot:ot + 1], None, ALU.mult), "ve")
                e.pbfree["pb%d" % (4 + p2)] = tvd
                e.need("gpsimd", tvd)
                dsp[ot] = e.dma("gpsimd", "sp", lambda ot=ot, c0=c0: scr["ffno"][ot * 128:(ot + 1) * 128, c0:c0 + 512], lambda: sb["d_out"][:])
                e.need("vector", dsp[ot])
                last_tvd = tvd
        e.t_ffn = e.tok("sp")


def emit_wo_final(e, inp, scr, nc, out_t):
    T, TO = e.T, e.TO
    with contextlib.ExitStack() as wes:
        sb = {}

        def wsb(name, shape, dt=F32):
            sb[name] = wes.enter_context(nc.sbuf_tensor(name, list(shape), dt))

        wsb("o_all", (128, 16 * 512))
        wsb("o_h", (128, 16 * 512), BF16); wsb("o_l", (128, 16 * 512), BF16)
        for p2 in range(4):
            wsb("o_wh%d" % p2, (128, HID), BF16); wsb("o_wl%d" % p2, (128, HID), BF16)
        wsb("o_acc", (128, 512))
        wsb("o_x", (128, 4 * HID))
        wsb("o_fin", (128, 512))
        wsb("o_ffn", (128, 512))

        pb = e.ps
        e.need("sync", e.t_cc, e.t_const, e.t_ffn, e.tok("ve"), e.tok("sc"), e.tok("gp"), *e.t_wsplit["wo"])
        e.op("sync", lambda E, S: E.reg_load(e.r_own, e.sb["c_offs"][0:1, 0:1]))
        e.need("tensor", e.t_ffn)
        e.bank(*["pb%d" % i for i in range(8)])

        prev_split = None
        prev_fin = {}
        for tc in range(TO // 512):
            c0 = tc * 512
            t_x = None
            if prev_fin:
                e.need("sync", prev_fin[max(prev_fin)])
            for j in range(4):
                t_x = e.dma("sync", "oxl", lambda j=j: sb["o_x"][:, j * HID:(j + 1) * HID],
                            lambda j=j, c0=c0: inp["xo"][c0 + j * 128:c0 + (j + 1) * 128, :])
            tspl = None
            e.need("sync", prev_split)
            e.op("sync", lambda E, S, c0=c0: E.reg_mov(e.r_tmp, c0))
            e.op("sync", lambda E, S: E.reg_add(e.r_tmp, e.r_tmp, e.r_own))
            tload = e.do("sync", lambda E, S, T=T: E.dma_start(
                out=sb["o_all"][:],
                in_=bass.AP(scr["cc_out"], e.r_tmp, [[T, 128], [T * 128, 16], [1, 512]])), "odl", 16)
            e.need("vector", tload)
            for f in range(16):
                e.do("vector", lambda E, S, f=f: E.tensor_copy(sb["o_h"][:, f * 512:(f + 1) * 512], sb["o_all"][:, f * 512:(f + 1) * 512]), "ve")
                e.need("vector", e.tok("ve"))
                tspl = e.do("vector", lambda E, S, f=f: E.tensor_tensor(sb["o_l"][:, f * 512:(f + 1) * 512], sb["o_all"][:, f * 512:(f + 1) * 512], sb["o_h"][:, f * 512:(f + 1) * 512], ALU.subtract), "ve")
            prev_split = tspl
            e.need("tensor", tspl, t_x)
            tacc_hist = {}
            for ot in range(16):
                p2 = ot % 4
                if tacc_hist.get(ot - 4) is not None:
                    e.need("sync", tacc_hist[ot - 4])
                twh = e.dma("sync", "ww%d" % p2, lambda p2=p2: sb["o_wh%d" % p2][:, :],
                            lambda ot=ot: bass.AP(scr["h_wo"], ot * 128, [[HID, 128], [HID * 128, 16], [1, 128]]))
                twl = e.dma("sync", "ww%d" % p2, lambda p2=p2: sb["o_wl%d" % p2][:, :],
                            lambda ot=ot: bass.AP(scr["l_wo"], ot * 128, [[HID, 128], [HID * 128, 16], [1, 128]]))
                e.need("tensor", twl)
                e.bank("pb%d" % p2)
                tmw = None
                for kt in range(16):
                    for pi, (wt, xt) in enumerate((("o_wh%d" % p2, "o_h"), ("o_wh%d" % p2, "o_l"), ("o_wl%d" % p2, "o_h"))):
                        tmw = e.do("tensor", lambda E, S, wt=wt, xt=xt, kt=kt, p2=p2, f=(kt == 0 and pi == 0), l=(kt == 15 and pi == 2):
                                   E.matmul(pb["pb%d" % p2][:, :], sb[wt][:, kt * 128:(kt + 1) * 128],
                                            sb[xt][:, kt * 512:(kt + 1) * 512], start=f, stop=l),
                                   "mm" if (kt == 15 and pi == 2) else None)
                e.need("sync", e.t_ffn)
                if tacc_hist.get(ot - 1) is not None:
                    e.need("sync", tacc_hist[ot - 1])
                tff = e.dma("sync", "ofl", lambda: sb["o_ffn"][:],
                            lambda ot=ot, c0=c0: scr["ffno"][ot * 128:(ot + 1) * 128, c0:c0 + 512])
                e.need("vector", tmw, tff, prev_fin.get(ot - 1))
                tacc = e.do("vector", lambda E, S, p2=p2: E.tensor_tensor(sb["o_acc"][:], pb["pb%d" % p2][:, :], sb["o_ffn"][:], ALU.add), "ve")
                e.pbfree["pb%d" % p2] = tacc
                tacc_hist[ot] = tacc
                e.need("tensor", tacc)
                e.bank("pb6")
                ttr = None
                for j in range(4):
                    ttr = e.do("tensor", lambda E, S, j=j: E.transpose(pb["pb6"][:, j * 128:(j + 1) * 128], sb["o_acc"][:, j * 128:(j + 1) * 128], e.sb["c_ident"][:]), "mm")
                e.need("vector", ttr)
                tf = None
                for j in range(4):
                    tf = e.do("vector", lambda E, S, j=j, ot=ot: E.tensor_tensor(sb["o_fin"][:, j * 128:(j + 1) * 128], pb["pb6"][:, j * 128:(j + 1) * 128],
                                                                                sb["o_x"][:, j * HID + ot * 128:j * HID + (ot + 1) * 128], ALU.add), "ve")
                e.pbfree["pb6"] = tf
                prev_fin[ot] = tf
                e.need("gpsimd", tf)
                tso = e.dma("gpsimd", "so", lambda ot=ot, c0=c0: bass.AP(out_t, c0 * HID + ot * 128, [[HID, 128], [HID * 128, 4], [1, 128]]),
                            lambda: sb["o_fin"][:])
                prev_fin[ot] = tso
        e.need("gpsimd", e.tok("so"))



def _mk_consts():
    ident = np.eye(128, dtype=np.float32)
    tri = np.zeros((128, 128), np.float32)
    mask = np.zeros((128, 128), np.float32)
    for c in range(4):
        for s in range(32):
            for t in range(s, 32):
                tri[c * 32 + s, c * 32 + t] = -1.0
                mask[c * 32 + s, c * 32 + t] = 1.0
    onesrow = np.ones((1, 128), np.float32)
    negcol = -np.ones((128, 1), np.float32)
    return ident, tri, mask, onesrow, negcol


def make_in_maps(x, Wq, Wk, Wv, Ww, cq, ck, cv, Wf1, Wf2, Wb, Wg1, Wg2, bg2, norm_w,
                 Wo, gw, uw, dw, gsc, usc, dsc):
    Bb, T, _ = x.shape
    TO = T // 2
    ident, tri, mask, onesrow, negcol = _mk_consts()

    def colpack(v, w):
        return np.ascontiguousarray(np.asarray(v, np.float32).reshape(w, 128).T)

    in_maps = []
    for core in range(8):
        b = core // 2
        hg = core % 2
        hs = slice(hg * HCOLS, (hg + 1) * HCOLS)
        own0 = hg * TO
        m = {
            "h": np.ascontiguousarray(x[b]),
            "xo": np.ascontiguousarray(x[b, own0:own0 + TO]),
            "wq": np.ascontiguousarray(Wq[:, hs]),
            "wk": np.ascontiguousarray(Wk[:, hs]),
            "wv": np.ascontiguousarray(Wv[:, hs]),
            "ww": np.ascontiguousarray(Ww[:, hs]),
            "cq": np.ascontiguousarray(cq[hs].reshape(NH, 128, 4).transpose(1, 0, 2).reshape(128, 32)),
            "ck": np.ascontiguousarray(ck[hs].reshape(NH, 128, 4).transpose(1, 0, 2).reshape(128, 32)),
            "cv": np.ascontiguousarray(cv[hs].reshape(NH, 128, 4).transpose(1, 0, 2).reshape(128, 32)),
            "wf1": Wf1, "wf2": np.ascontiguousarray(Wf2[:, hs]),
            "wb": np.ascontiguousarray(Wb[:, hg * NH:(hg + 1) * NH]),
            "wg1": Wg1, "wg2": np.ascontiguousarray(Wg2[:, hs]),
            "bg2b": np.ascontiguousarray(np.broadcast_to(bg2[hs][None, :], (128, HCOLS))),
            "normwb": np.ascontiguousarray(np.broadcast_to(np.tile(norm_w, 4)[None, :], (128, 512))),
            "wo": Wo, "gw": gw, "uw": uw, "dw": dw,
            "gsc": colpack(gsc, FFI // 128),
            "usc": colpack(usc, FFI // 128),
            "dsc": colpack(dsc, HID // 128),
            "ident": ident, "trineg": tri, "maskst": mask,
            "onesrow": onesrow, "negcol": negcol,
            "offs": np.array([[own0, own0 * HID, 0, 0, 0, 0, 0, 0]], np.int32),
            "cvals": np.ascontiguousarray(np.broadcast_to(np.array([EPS_LN, EPS_L2, 1.0, -1.0], np.float32)[None, :], (128, 4))),
        }
        in_maps.append(m)
    return in_maps


def kernel(L0_attn_hidden_states, L0_attn_Wq, L0_attn_Wk, L0_attn_Wv, L0_attn_Ww,
           L0_attn_cq, L0_attn_ck, L0_attn_cv, L0_attn_Wf1, L0_attn_Wf2, L0_attn_Wb,
           L0_attn_Wg1, L0_attn_Wg2, L0_attn_bg2, L0_attn_norm_w, L0_attn_Wo,
           L0_ffn_gate_w, L0_ffn_up_w, L0_ffn_down_w, L0_ffn_gate_scale,
           L0_ffn_up_scale, L0_ffn_down_scale):
    f32 = lambda a: np.ascontiguousarray(np.asarray(a, dtype=np.float32))
    x = f32(L0_attn_hidden_states)
    Bb, T, _ = x.shape
    TO = T // 2
    in_maps = make_in_maps(x, f32(L0_attn_Wq), f32(L0_attn_Wk), f32(L0_attn_Wv), f32(L0_attn_Ww),
                           f32(L0_attn_cq), f32(L0_attn_ck), f32(L0_attn_cv), f32(L0_attn_Wf1),
                           f32(L0_attn_Wf2), f32(L0_attn_Wb), f32(L0_attn_Wg1), f32(L0_attn_Wg2),
                           f32(L0_attn_bg2), f32(L0_attn_norm_w), f32(L0_attn_Wo),
                           f32(L0_ffn_gate_w), f32(L0_ffn_up_w), f32(L0_ffn_down_w),
                           f32(L0_ffn_gate_scale), f32(L0_ffn_up_scale), f32(L0_ffn_down_scale))
    nc = build_program(T)
    res = run_bass_kernel_spmd(nc, in_maps, core_ids=list(range(8)))
    out = np.empty((Bb, T, HID), np.float32)
    for core in range(8):
        b = core // 2
        hg = core % 2
        out[b, hg * TO:(hg + 1) * TO] = res.results[core]["out"]
    return out
